# version 07 (final)
# speedup vs baseline: 1.1857x; 1.0360x over previous
import contextlib
import numpy as np

import concourse.bass as bass
import concourse.mybir as mybir
from concourse.bass_utils import run_bass_kernel_spmd

F32 = mybir.dt.float32
BF16 = mybir.dt.bfloat16
I32 = mybir.dt.int32
F8E5 = mybir.dt.float8e5
ALU = mybir.AluOpType
ACT = mybir.ActivationFunctionType

C = 1024
NC = 8
H4 = 4096
NH = 32
NT = 512
NTILES = 8
MAIN0 = 4
NSUB = 4

M_SBIN = -8388608
M_BF16 = -65536
M_E5M2 = -2097152
M_TIE = 4128768
V_TIE = 1048576.0


class Plan:
    ENGS = ("sync", "gpsimd", "vector", "scalar", "tensor")

    def __init__(self, nc, es):
        self.nc = nc
        self.lists = {e: [] for e in self.ENGS}
        self.sems = {}
        self.cnt = {}
        self.es = es

    def sem(self, name):
        self.sems[name] = self.es.enter_context(self.nc.semaphore(name))
        self.cnt[name] = 0

    def val(self, name):
        return self.cnt[name]

    AUTOF = ("tensor_reduce", "tensor_tensor_scan", "reciprocal")

    def op(self, eng, meth, *args, wait=None, inc=None, incby=1, **kw):
        if inc is not None:
            self.cnt[inc] += incby
        sems = self.sems
        wlist = tuple((s, v) for (s, v) in (wait or ()) if v and v > 0)

        def fn(E, meth=meth, args=args, kw=kw, wlist=wlist, inc=inc, incby=incby):
            for s, v in wlist:
                E.wait_ge(sems[s], v)
            instr = getattr(E, meth)(*args, **kw)
            if inc is not None:
                instr.then_inc(sems[inc], incby)

        self.lists[eng].append(fn)
        if eng in ("vector", "scalar") and inc is not None:
            fence = meth in self.AUTOF
            if not fence and meth == "activation" and kw.get("accum_out") is not None:
                fence = True
            if not fence:
                out_ap = kw.get("out")
                if out_ap is None and args:
                    out_ap = args[0]
                try:
                    if out_ap is not None and out_ap.free_size() < 256:
                        fence = True
                except Exception:
                    pass
            if fence:
                self.raw_wait(eng, inc, self.cnt[inc])
        return self.cnt[inc] if inc is not None else None

    def raw_wait(self, eng, sem, value):
        if value <= 0:
            return
        sems = self.sems

        def fn(E, sem=sem, value=value):
            E.wait_ge(sems[sem], value)

        self.lists[eng].append(fn)


def _build(debug_taps=False):
    nc = bass.Bass("TRN2", target_bir_lowering=False, debug=False)

    dr = lambda n, sh, dt, k: nc.dram_tensor(n, sh, dt, kind=k).ap()
    x_in = dr("x_in", [4096, C], F32, "ExternalInput")
    xe_in = dr("xe_in", [128, NC], F32, "ExternalInput")
    mask_in = dr("mask_in", [128, 4, NC, NT], BF16, "ExternalInput")
    maske_in = dr("maske_in", [128, NC], BF16, "ExternalInput")
    ident_in = dr("ident_in", [128, 128], F32, "ExternalInput")
    rho_in = dr("rho_in", [128, 1], F32, "ExternalInput")
    sig_in = dr("sig_in", [128, 1], F32, "ExternalInput")
    cvec_in = dr("cvec_in", [128, 8, NC], F32, "ExternalInput")
    b1_in = dr("b1_in", [128, NH], F32, "ExternalInput")
    b2_in = dr("b2_in", [128, NC], F32, "ExternalInput")
    wkT_in = dr("wkT_in", [128, 2, NC, NT], F32, "ExternalInput")
    wvT_in = dr("wvT_in", [128, 2, NC, NT], F32, "ExternalInput")
    wrT_in = dr("wrT_in", [128, 2, NC, NT], F32, "ExternalInput")
    wcrT_in = dr("wcrT_in", [128, 2, NC, NT], F32, "ExternalInput")
    wckT_in = dr("wckT_in", [128, 8, NC, NT], F32, "ExternalInput")
    wcvT_in = dr("wcvT_in", [128, 8, NC, NT], F32, "ExternalInput")
    w1T_in = dr("w1T_in", [128, 8, NC, NT], F32, "ExternalInput")
    w2T_in = dr("w2T_in", [128, 8, NC, NT], F32, "ExternalInput")

    out_main = dr("out_main", [2048, C], F32, "ExternalOutput")
    out_extra = dr("out_extra", [128, NC], F32, "ExternalOutput")

    PSZ = NC * NT
    wk_s = dr("wk_s", [128, 2 * PSZ], BF16, "Internal")
    wv_s = dr("wv_s", [128, 2 * PSZ], BF16, "Internal")
    wr_s = dr("wr_s", [128, 2 * PSZ], BF16, "Internal")
    wcr_s = dr("wcr_s", [128, 2 * PSZ], BF16, "Internal")
    wck_s = dr("wck_s", [128, 8 * PSZ], BF16, "Internal")
    wcv_s = dr("wcv_s", [128, 8 * PSZ], BF16, "Internal")
    w1_s = dr("w1_s", [128, 8 * PSZ], BF16, "Internal")
    w2_s = dr("w2_s", [128, 8 * PSZ], BF16, "Internal")

    taps = {}
    if debug_taps:
        for n, sh in [("tp_xn", [128, NC, NT]), ("tp_ek", [128, NC, NT]),
                      ("tp_B", [128, NC, NT]), ("tp_aa", [128, NC, NT]),
                      ("tp_x1", [128, NC, NT]), ("tp_hq", [128, NH, NT]),
                      ("tp_g", [128, NC, NT]), ("tp_h2", [128, NC, NT]),
                      ("tp_s", [128, NC, NT])]:
            taps[n] = dr(n, sh, F32, "ExternalOutput")
        taps["tp_ec"] = dr("tp_ec", [128, 10, NH], F32, "ExternalOutput")
        taps["tp_carB"] = dr("tp_carB", [128, NC], F32, "ExternalOutput")
        taps["tp_carA"] = dr("tp_carA", [128, NC], F32, "ExternalOutput")
        taps["tp_hcar"] = dr("tp_hcar", [128, NC], F32, "ExternalOutput")
        taps["tp_h2car"] = dr("tp_h2car", [128, NC], F32, "ExternalOutput")

    es = contextlib.ExitStack()
    sb = lambda n, sh, dt: es.enter_context(nc.sbuf_tensor(n, sh, dt))
    psb = lambda n, sh: es.enter_context(nc.psum_tensor(n, sh, F32))

    with es:
        ident = sb("ident", [128, 128], F32)
        ones_sq = sb("ones_sq", [128, 128], F32)
        ones512 = sb("ones512", [128, NT], F32)
        cvec = sb("cvec", [128, 8, NC], F32)
        omts = sb("omts", [128, NC], F32)
        eu = sb("eu", [128, NC], F32)
        rho = sb("rho", [128, 1], F32)
        sig = sb("sig", [128, 1], F32)
        b1q = sb("b1q", [128, NH], F32)
        b2q = sb("b2q", [128, NC], F32)
        bqi = sb("bqi", [128, NH], I32)
        carB = sb("carB", [128, NC], F32)
        carA = sb("carA", [128, NC], F32)
        hcar = sb("hcar", [128, NC], F32)
        h2car = sb("h2car", [128, NC], F32)
        bc_xn = sb("bc_xn", [128, NC], F32)
        bc_ek = sb("bc_ek", [128, NC], F32)
        bc_v = sb("bc_v", [128, NC], F32)
        bc_B = sb("bc_B", [128, NC], F32)
        bc_aa = sb("bc_aa", [128, NC], F32)
        bc_xxx = sb("bc_xxx", [128, NC], BF16)
        ec = sb("ec", [128, 10, NH], F32)
        ecb = sb("ecb", [128, 2, NH], BF16)
        st1 = sb("st1", [1, NT], F32)
        st2 = sb("st2", [1, NT], F32)
        st3 = sb("st3", [1, NT], F32)
        stat = sb("stat", [128, 8], F32)
        epsb = sb("epsb", [128, 1], F32)
        xs = [sb("xs0", [128, 1024], F32)]
        xr = [sb(f"xr{i}", [128, 1024], F32) for i in range(2)]
        xnS = sb("xnS", [128, 1024], F32)
        pc = [sb(f"pc{i}", [128, NC, NT], BF16) for i in range(3)]
        xnF = sb("xnF", [128, NC, NT + 1], F32)
        hx1 = sb("hx1", [128, NC, NT + 1], F32)
        xxb = sb("xxb", [128, NC, NT], BF16)
        P0 = sb("P0", [128, NC, NT], F32)
        P1 = sb("P1", [128, NC, NT], F32)
        P2 = sb("P2", [128, NC, NT], F32)
        gF = sb("gF", [128, NC, NT], F32)
        Bbuf = sb("Bbuf", [128, NT + 1], F32)
        Abuf = sb("Abuf", [128, NT + 1], F32)
        wsc = sb("wsc", [128, 3, NT], F32)
        hqkc = sb("hqkc", [128, NH, NT], BF16)
        xqb = sb("xqb", [128, NC, NT], BF16)
        f8q = sb("f8q", [128, 4, NT], F8E5)

        PSA = psb("PSA", [128, NT])
        PSB = psb("PSB", [128, NT])
        PSC = psb("PSC", [128, NT])
        PSD = psb("PSD", [128, NT])
        PSE = psb("PSE", [128, NT])
        PSF = psb("PSF", [128, NT])
        PST = psb("PST", [128, NT])
        PSX = psb("PSX", [128, NT])
        BANKS8 = [PSA, PSB, PSC, PSD, PSE, PSF, PST, PSX]
        BNAMES = ["PSA", "PSB", "PSC", "PSD", "PSE", "PSF", "PST", "PSX"]

        p = Plan(nc, es)
        for s in ("T", "V", "S", "LC", "LX0", "LX1", "LR0", "LR1",
                  "LP0", "LP1", "LP2", "LM", "GS", "GO", "GD"):
            p.sem(s)

        V, S, T = "vector", "scalar", "tensor"

        bank_free = {b: None for b in BNAMES}

        def bf_wait(*banks):
            w = []
            for b in banks:
                if bank_free[b] is not None:
                    w.append(bank_free[b])
            return w

        for (dst, src) in [(ident[:, :], ident_in), (cvec[:, :, :], cvec_in),
                           (rho[:, :], rho_in), (sig[:, :], sig_in),
                           (b1q[:, :], b1_in), (b2q[:, :], b2_in)]:
            p.op("sync", "dma_start", inc="LC", incby=16, out=dst, in_=src)
        lc_consts = p.val("LC")

        p.op(V, "memset", ones_sq[:, :], 1.0, inc="V")
        p.op(V, "memset", epsb[:, :], 1e-5, inc="V")
        p.op(V, "memset", ones512[:, :], 1.0, inc="V")
        p.op(V, "memset", carB[:, :], 1.0, inc="V")
        p.op(V, "memset", carA[:, :], 0.0, inc="V")

        p.op(V, "tensor_scalar", wait=[("LC", lc_consts)],
             out=omts[:, :], in0=cvec[:, 0, :], scalar1=-1.0, scalar2=1.0,
             op0=ALU.mult, op1=ALU.add, inc="V")
        p.op(S, "activation", wait=[("LC", lc_consts)],
             out=eu[:, :], in_=cvec[:, 5, :], func=ACT.Exp, inc="S")

        def fq_half_inplace(buf, ibuf, n):
            p.op(V, "tensor_scalar", out=ibuf[:, 0:n], in0=buf[:, 0:n].bitcast(I32),
                 scalar1=M_SBIN, scalar2=None, op0=ALU.bitwise_and, inc="V")
            p.op(V, "scalar_tensor_tensor", out=buf[:, 0:n], in0=ibuf[:, 0:n].bitcast(F32),
                 scalar=0.125, in1=buf[:, 0:n], op0=ALU.mult, op1=ALU.add, inc="V")
            p.op(V, "tensor_scalar", out=ibuf[:, 0:n], in0=buf[:, 0:n].bitcast(I32),
                 scalar1=M_E5M2, scalar2=None, op0=ALU.bitwise_and, inc="V")
            p.op(V, "tensor_copy", out=buf[:, 0:n], in_=ibuf[:, 0:n].bitcast(F32), inc="V")
        fq_half_inplace(b1q, bqi, NH)
        fq_half_inplace(b2q, bqi, NC)

        gs_ready = {}
        cast_st = {}
        cast_jobs = [("wk", wkT_in, wk_s), ("wv", wvT_in, wv_s),
                     ("wr", wrT_in, wr_s), ("wcr", wcrT_in, wcr_s),
                     ("wck", wckT_in, wck_s), ("wcv", wcvT_in, wcv_s)]
        cj = 0
        for (key, src_, dst) in cast_jobs:
            for pi in range(src_.shape[1]):
                par = cj % 2
                bb = hqkc[:, par * 8:(par + 1) * 8, :]
                p.op("gpsimd", "dma_start", wait=[("GS", cast_st.get(cj - 2, 0))],
                     inc="GD", incby=16, out=bb, in_=src_[:, pi, :, :])
                p.op("sync", "dma_start", wait=[("GD", p.val("GD"))], inc="GS", incby=16,
                     out=dst[:, pi * PSZ:(pi + 1) * PSZ],
                     in_=bb.rearrange("p a b -> p (a b)"))
                cast_st[cj] = p.val("GS")
                cj += 1
            gs_ready[key] = p.val("GS")

        w12_st = {}

        def emit_w12_jobs(key, src_, dst):
            stage = [(P0, P1), (P2, gF)]
            bbs = [hqkc[:, 16:24, :], hqkc[:, 24:32, :]]
            s_now = p.val("S")
            v_now = p.val("V")
            for si in range(src_.shape[1]):
                par = si % 2
                st, sc = stage[par]
                pv = st[:, :, :].rearrange("p a b -> p (a b)")
                p1v = sc[:, :, :].rearrange("p a b -> p (a b)")
                bfv = bbs[par].rearrange("p a b -> p (a b)")
                lsem = "LC" if par == 0 else "GD"
                p.op("gpsimd", "dma_start",
                     wait=[("S", s_now if si < 2 else 0),
                           ("V", v_now if si < 2 else w12_st.get(si - 2, 0))],
                     inc=lsem, incby=16, out=st[:, :, :], in_=src_[:, si, :, :])
                lcv = p.val(lsem)
                for h in range(2):
                    p.op(V, "tensor_copy",
                         wait=[(lsem, lcv)] if h == 0 else None,
                         out=f8q[:, :, :], in_=st[:, h * 4:(h + 1) * 4, :], inc="V")
                    p.op(V, "tensor_copy",
                         wait=[("GS", w12_st.get((key, si - 2), 0))] if h == 0 else None,
                         out=bbs[par][:, h * 4:(h + 1) * 4, :], in_=f8q[:, :, :], inc="V")
                w12_st[si] = p.val("V")
                p.op("sync", "dma_start", wait=[("V", p.val("V"))], inc="GS", incby=16,
                     out=dst[:, si * PSZ:(si + 1) * PSZ], in_=bfv[:, 0:NC * NT])
                w12_st[(key, si)] = p.val("GS")
            gs_ready[key] = p.val("GS")

        pstate = {"n": 0}
        t_after_piece = {}

        def load_piece(key, view):
            n = pstate["n"]
            pstate["n"] += 1
            slot = n % 3
            sem = ("LP0", "LP1", "LP2")[slot]
            w = [("T", t_after_piece.get(n - 2, 0)), ("GS", gs_ready[key])]
            out = pc[slot][:, :, :]
            if len(view.shape) == 2:
                out = out.rearrange("p a b -> p (a b)")
            p.op("sync", "dma_start", wait=w, inc=sem, incby=16, out=out, in_=view)
            return n, slot, (sem, p.val(sem))

        def piece_done(n):
            t_after_piece[n] = p.val("T")

        sub_state = {"t_z": 0, "s_last": 0}

        def ln_subtile(tile, s, xsbuf, wait_ld, extra_s_wait=0):
            c0 = s * 128
            prev_tz = sub_state["t_z"]
            p.op(V, "tensor_reduce", wait=[wait_ld, ("T", prev_tz)],
                 out=stat[:, 0:1], in_=xsbuf[:, :],
                 axis=mybir.AxisListType.X, op=ALU.add, inc="V")
            p.op(V, "tensor_scalar", out=stat[:, 0:1], in0=stat[:, 0:1],
                 scalar1=1.0 / C, scalar2=None, op0=ALU.mult, inc="V")
            p.op(V, "tensor_scalar", out=xsbuf[:, :], in0=xsbuf[:, :],
                 scalar1=stat[:, 0:1], scalar2=None, op0=ALU.subtract, inc="V")
            vv0 = p.val("V")
            p.op(S, "activation", wait=[("V", vv0), ("T", prev_tz),
                                        ("S", extra_s_wait)],
                 out=xnS[:, :], in_=xsbuf[:, :],
                 func=ACT.Square, accum_out=stat[:, 1:2], inc="S")
            p.op(S, "activation", out=stat[:, 2:3], in_=stat[:, 1:2], func=ACT.Sqrt,
                 scale=1.0 / C, bias=epsb[:, 0:1], inc="S")
            ss0 = p.val("S")
            p.op(V, "reciprocal", wait=[("S", ss0)], out=stat[:, 3:4], in_=stat[:, 2:3], inc="V")
            p.op(V, "tensor_scalar", out=xnS[:, :], in0=xsbuf[:, :],
                 scalar1=stat[:, 3:4], scalar2=None, op0=ALU.mult, inc="V")
            v_xn = p.val("V")
            for cj in range(NC):
                bank = PST if cj < 4 else PSX
                col = (cj % 4) * 128
                w = None
                if cj == 0:
                    w = [("V", v_xn)] + bf_wait("PST", "PSX")
                p.op(T, "transpose", wait=w,
                     out=bank[:, col:col + 128], in_=xnS[:, cj * 128:(cj + 1) * 128],
                     identity=ident[:, :], inc="T")
            t_xn = p.val("T")
            p.op(S, "copy", wait=[("T", t_xn)],
                 out=xnF[:, 0:4, c0:c0 + 128],
                 in_=PST[:, :].rearrange("p (a b) -> p a b", a=4), inc="S")
            p.op(S, "copy",
                 out=xnF[:, 4:8, c0:c0 + 128],
                 in_=PSX[:, :].rearrange("p (a b) -> p a b", a=4), inc="S")
            s_xn_ev = p.val("S")
            bank_free["PST"] = ("S", s_xn_ev)
            bank_free["PSX"] = ("S", s_xn_ev)
            p.op(V, "tensor_reduce", wait=[("T", t_xn)], out=stat[:, 4:5], in_=xnS[:, :],
                 axis=mybir.AxisListType.X, op=ALU.add, inc="V")
            p.op(V, "tensor_scalar", out=stat[:, 4:5], in0=stat[:, 4:5],
                 scalar1=1.0 / C, scalar2=None, op0=ALU.mult, inc="V")
            p.op(V, "tensor_scalar", out=xnS[:, :], in0=xnS[:, :],
                 scalar1=stat[:, 4:5], scalar2=None, op0=ALU.subtract, inc="V")
            vv1 = p.val("V")
            p.op(S, "activation", wait=[("V", vv1)], out=xsbuf[:, :], in_=xnS[:, :],
                 func=ACT.Square, accum_out=stat[:, 5:6], inc="S")
            p.op(S, "activation", out=stat[:, 6:7], in_=stat[:, 5:6], func=ACT.Sqrt,
                 scale=1.0 / C, bias=epsb[:, 0:1], inc="S")
            ss1 = p.val("S")
            sub_state["s_last"] = ss1
            p.op(V, "reciprocal", wait=[("S", ss1)], out=stat[:, 7:8], in_=stat[:, 6:7], inc="V")
            p.op(V, "tensor_scalar", out=xnS[:, :], in0=xnS[:, :],
                 scalar1=stat[:, 7:8], scalar2=None, op0=ALU.mult, inc="V")
            v_z = p.val("V")
            for cj in range(NC):
                bank = PST if cj < 4 else PSX
                col = (cj % 4) * 128
                w = None
                if cj == 0:
                    w = [("V", v_z)] + bf_wait("PST", "PSX")
                p.op(T, "transpose", wait=w,
                     out=bank[:, col:col + 128], in_=xnS[:, cj * 128:(cj + 1) * 128],
                     identity=ident[:, :], inc="T")
            t_z = p.val("T")
            sub_state["t_z"] = t_z
            for cj in range(NC):
                bank = PST if cj < 4 else PSX
                col = (cj % 4) * 128
                p.op(S, "activation", wait=[("T", t_z)] if cj == 0 else None,
                     out=hx1[:, cj, 1 + c0:1 + c0 + 128], in_=bank[:, col:col + 128],
                     func=ACT.Identity, scale=cvec[:, 1, cj:cj + 1], bias=cvec[:, 2, cj:cj + 1],
                     inc="S")
            s_h = p.val("S")
            bank_free["PST"] = ("S", s_h)
            bank_free["PSX"] = ("S", s_h)
            return s_h

        def quant_y1(srcv, dstv_bf, scr1, scr2, wait_s):
            p.op(V, "tensor_scalar", wait=[wait_s], out=scr1.bitcast(I32),
                 in0=srcv.bitcast(I32), scalar1=M_SBIN, scalar2=None,
                 op0=ALU.bitwise_and, inc="V")
            p.op(V, "scalar_tensor_tensor", out=scr2, in0=scr1.bitcast(F32),
                 scalar=0.00390625, in1=srcv, op0=ALU.mult, op1=ALU.add, inc="V")
            p.op(V, "tensor_scalar", out=scr1.bitcast(I32), in0=scr2.bitcast(I32),
                 scalar1=M_BF16, scalar2=None, op0=ALU.bitwise_and, inc="V")
            for h in range(2):
                p.op(V, "tensor_scalar", out=f8q[:, :, :],
                     in0=scr1[:, h * 4:(h + 1) * 4, :].bitcast(F32),
                     scalar1=0.0, scalar2=None, op0=ALU.max, inc="V")
                p.op(V, "tensor_copy", out=dstv_bf[:, h * 4:(h + 1) * 4, :],
                     in_=f8q[:, :, :], inc="V")
            return p.val("V")

        def rm_half(srcv, dstv, scr1, wait_s):
            p.op(V, "tensor_scalar", wait=[wait_s], out=scr1.bitcast(I32),
                 in0=srcv.bitcast(I32), scalar1=M_SBIN, scalar2=None,
                 op0=ALU.bitwise_and, inc="V")
            p.op(V, "scalar_tensor_tensor", out=dstv, in0=scr1.bitcast(F32),
                 scalar=0.00390625, in1=srcv, op0=ALU.mult, op1=ALU.add, inc="V")
            p.op(V, "tensor_scalar", out=dstv.bitcast(I32), in0=dstv.bitcast(I32),
                 scalar1=M_BF16, scalar2=None, op0=ALU.bitwise_and, inc="V")
            return p.val("V")

        def fq_half(srcv, dstv_bf, scr1, scr2, wait_list):
            for h in range(2):
                p.op(V, "tensor_copy", wait=wait_list if h == 0 else None,
                     out=f8q[:, :, :], in_=srcv[:, h * 4:(h + 1) * 4, :], inc="V")
                p.op(V, "tensor_copy", out=dstv_bf[:, h * 4:(h + 1) * 4, :],
                     in_=f8q[:, :, :], inc="V")
            return p.val("V")

        ts_ap = lambda ci: cvec[:, 0, ci:ci + 1]
        eu_ap = lambda ci: eu[:, ci:ci + 1]
        omts_ap = lambda ci: omts[:, ci:ci + 1]

        p.op(V, "tensor_copy", wait=[("LC", lc_consts)],
             out=hx1[:, :, 0:1], in_=cvec[:, 0, :].unsqueeze(-1), inc="V")

        xsub_done = {}
        xout_done = {}
        osub_go = {0: 0, 1: 0}
        xr_last = {0: 0, 1: 0}
        mask_done_v = 0
        xxb_free_t = 0
        prev_tail_v = 0
        prev_xnF_free = 0
        prev_hx1_free = 0
        prev_g_free = 0

        for tile in range(NTILES):
            is_main = tile >= MAIN0
            if tile == 1:
                emit_w12_jobs("w1", w1T_in, w1_s)
            elif tile == 2:
                emit_w12_jobs("w2", w2T_in, w2_s)

            h_evac_s = 0
            for s in range(NSUB):
                par = s % 2
                sem = "LX0" if par == 0 else "LX1"
                row0 = tile * NT + s * 128
                wv = [("V", xsub_done.get((tile, s - 1), 0) if s > 0 else
                       (xsub_done.get((tile - 1, NSUB - 1), 0))),
                      ("S", sub_state["s_last"] if tile + s > 0 else 0),
                      ("V", prev_tail_v if s == 0 else 0)]
                p.op("sync", "dma_start", wait=wv, inc=sem, incby=16,
                     out=xs[0][:, :], in_=x_in[row0:row0 + 128, :])
                ldw = (sem, p.val(sem))
                extra_s = 0
                if s == 0:
                    p.raw_wait(S, "V", prev_xnF_free)
                    p.raw_wait(S, "V", prev_hx1_free)
                h_evac_s = ln_subtile(tile, s, xs[0], ldw)
                xsub_done[(tile, s)] = p.val("V")

            if tile == MAIN0:
                p.raw_wait(V, "GS", p.val("GS"))
                p.op(V, "tensor_scalar", wait=[("S", h_evac_s)], out=carB[:, :], in0=carB[:, :],
                     scalar1=rho[:, 0:1], scalar2=None, op0=ALU.mult, inc="V")
                p.op(V, "tensor_scalar", out=carB[:, :], in0=carB[:, :],
                     scalar1=sig[:, 0:1], scalar2=None, op0=ALU.add, inc="V")
                p.op(V, "tensor_scalar", out=carA[:, :], in0=carA[:, :],
                     scalar1=rho[:, 0:1], scalar2=None, op0=ALU.mult, inc="V")
                p.op(V, "tensor_scalar", out=hcar[:, :], in0=hcar[:, :],
                     scalar1=rho[:, 0:1], scalar2=None, op0=ALU.mult, inc="V")
                p.op(V, "scalar_tensor_tensor", out=hcar[:, :], in0=cvec[:, 0, :],
                     scalar=sig[:, 0:1], in1=hcar[:, :], op0=ALU.mult, op1=ALU.add, inc="V")
                p.op(V, "tensor_copy", out=hx1[:, :, 0:1], in_=hcar[:, :].unsqueeze(-1), inc="V")
            elif tile > 0:
                p.op(V, "tensor_copy", wait=[("S", h_evac_s)],
                     out=hx1[:, :, 0:1], in_=hcar[:, :].unsqueeze(-1), inc="V")
            for ci in range(NC):
                w = None
                if ci == 0:
                    w = [("S", h_evac_s), ("T", xxb_free_t)]
                p.op(V, "tensor_tensor", wait=w, out=P0[:, ci, :],
                     in0=hx1[:, ci, 1:NT + 1], in1=hx1[:, ci, 0:NT], op=ALU.subtract, inc="V")
                p.op(V, "scalar_tensor_tensor", out=xxb[:, ci, :], in0=P0[:, ci, :],
                     scalar=ts_ap(ci), in1=hx1[:, ci, 1:NT + 1], op0=ALU.mult, op1=ALU.add,
                     inc="V")
            v_xxx = p.val("V")
            p.op(V, "tensor_copy", out=hcar[:, :].unsqueeze(-1), in_=hx1[:, :, NT:NT + 1], inc="V")

            P0f = P0[:, :, :]
            P1f = P1[:, :, :]
            P2f = P2[:, :, :]
            if is_main:
                v_xq = fq_half(xnF[:, :, 0:NT], xqb[:, :, :], P1f, P2f,
                               [("S", h_evac_s)])

            slot_free = {0: 0, 1: 0}
            Bv = Bbuf[:, 0:NT]
            Av = Abuf[:, 0:NT]
            for oh in range(2):
                nk, wk_slot, wk_w = load_piece("wk", wk_s[:, oh * PSZ:(oh + 1) * PSZ])
                nv, wv_slot, wv_w = load_piece("wv", wv_s[:, oh * PSZ:(oh + 1) * PSZ])
                if is_main:
                    nr, wr_slot, wr_w = load_piece("wr", wr_s[:, oh * PSZ:(oh + 1) * PSZ])
                for ci4 in range(4):
                    ci = oh * 4 + ci4
                    par = ci % 2
                    ekS = P0[:, 0 + par, :]
                    vS = P0[:, 2 + par, :]
                    rS = P0[:, 4 + par, :]
                    w = ([wk_w] + ([("V", v_xxx)] if ci == 0 else [])) if ci4 == 0 else []
                    w = w + bf_wait("PSA")
                    for kk in range(NC):
                        p.op(T, "matmul", wait=w if kk == 0 else None,
                             out=PSA[:, :], lhsT=pc[wk_slot][:, kk, ci4 * 128:(ci4 + 1) * 128],
                             rhs=xxb[:, kk, :], start=(kk == 0), stop=(kk == NC - 1), inc="T")
                    t_k = p.val("T")
                    w = ([wv_w] if ci4 == 0 else []) + bf_wait("PSB")
                    for kk in range(NC):
                        p.op(T, "matmul", wait=w if kk == 0 else None,
                             out=PSB[:, :], lhsT=pc[wv_slot][:, kk, ci4 * 128:(ci4 + 1) * 128],
                             rhs=xxb[:, kk, :], start=(kk == 0), stop=(kk == NC - 1), inc="T")
                    t_v = p.val("T")
                    if is_main:
                        w = ([wr_w] if ci4 == 0 else []) + bf_wait("PSC")
                        for kk in range(NC):
                            p.op(T, "matmul", wait=w if kk == 0 else None,
                                 out=PSC[:, :], lhsT=pc[wr_slot][:, kk, ci4 * 128:(ci4 + 1) * 128],
                                 rhs=xxb[:, kk, :], start=(kk == 0), stop=(kk == NC - 1), inc="T")
                        t_r = p.val("T")
                    p.op(S, "activation", wait=[("T", t_k), ("V", slot_free[par])],
                         out=ekS, in_=PSA[:, :], func=ACT.Exp, inc="S")
                    bank_free["PSA"] = ("S", p.val("S"))
                    p.op(S, "copy", wait=[("T", t_v)], out=vS, in_=PSB[:, :], inc="S")
                    bank_free["PSB"] = ("S", p.val("S"))
                    if is_main:
                        p.op(S, "activation", wait=[("T", t_r)], out=rS, in_=PSC[:, :],
                             func=ACT.Sigmoid, inc="S")
                        bank_free["PSC"] = ("S", p.val("S"))
                    s_evac = p.val("S")
                    p.op(V, "tensor_copy", wait=[("S", s_evac)],
                         out=Bbuf[:, 0:1], in_=carB[:, ci:ci + 1], inc="V")
                    p.op(V, "tensor_tensor_scan", out=Bbuf[:, 1:NT + 1], data0=ones512[:, :],
                         data1=ekS, initial=Bbuf[:, 0:1], op0=ALU.mult, op1=ALU.add, inc="V")
                    p.op(V, "tensor_copy", out=carB[:, ci:ci + 1], in_=Bbuf[:, NT:NT + 1], inc="V")
                    p.op(V, "tensor_tensor", out=wsc[:, 0, :], in0=Bv, in1=ekS, op=ALU.max, inc="V")
                    p.op(V, "reciprocal", out=wsc[:, 1, :], in_=wsc[:, 0, :], inc="V")
                    p.op(V, "tensor_tensor", out=wsc[:, 0, :], in0=Bv, in1=wsc[:, 1, :],
                         op=ALU.mult, inc="V")
                    p.op(V, "tensor_tensor", out=wsc[:, 1, :], in0=ekS, in1=wsc[:, 1, :],
                         op=ALU.mult, inc="V")
                    p.op(V, "tensor_tensor", out=wsc[:, 1, :], in0=wsc[:, 1, :], in1=vS,
                         op=ALU.mult, inc="V")
                    p.op(V, "tensor_copy", out=Abuf[:, 0:1], in_=carA[:, ci:ci + 1], inc="V")
                    p.op(V, "tensor_tensor_scan", out=Abuf[:, 1:NT + 1], data0=wsc[:, 0, :],
                         data1=wsc[:, 1, :], initial=Abuf[:, 0:1], op0=ALU.mult, op1=ALU.add, inc="V")
                    p.op(V, "tensor_copy", out=carA[:, ci:ci + 1], in_=Abuf[:, NT:NT + 1], inc="V")
                    if tile == MAIN0 - 1:
                        p.op(V, "tensor_copy", out=bc_ek[:, ci:ci + 1], in_=ekS[:, NT - 1:NT], inc="V")
                        p.op(V, "tensor_copy", out=bc_v[:, ci:ci + 1], in_=vS[:, NT - 1:NT], inc="V")
                        p.op(V, "tensor_copy", out=bc_B[:, ci:ci + 1], in_=Bbuf[:, NT - 1:NT], inc="V")
                        p.op(V, "tensor_copy", out=bc_aa[:, ci:ci + 1], in_=Abuf[:, NT - 1:NT], inc="V")
                    if is_main:
                        p.op(V, "tensor_scalar", out=ekS, in0=ekS, scalar1=eu_ap(ci),
                             scalar2=None, op0=ALU.mult, inc="V")
                        p.op(V, "tensor_tensor", out=wsc[:, 0, :], in0=Bv, in1=ekS, op=ALU.add, inc="V")
                        p.op(V, "reciprocal", out=wsc[:, 1, :], in_=wsc[:, 0, :], inc="V")
                        p.op(V, "tensor_tensor", out=wsc[:, 2, :], in0=Bv, in1=Av, op=ALU.mult, inc="V")
                        p.op(V, "tensor_tensor", out=ekS, in0=ekS, in1=vS, op=ALU.mult, inc="V")
                        p.op(V, "tensor_tensor", out=wsc[:, 2, :], in0=wsc[:, 2, :], in1=ekS, op=ALU.add, inc="V")
                        p.op(V, "tensor_tensor", out=wsc[:, 2, :], in0=wsc[:, 2, :], in1=wsc[:, 1, :], op=ALU.mult, inc="V")
                        p.op(V, "tensor_tensor", out=wsc[:, 2, :], in0=wsc[:, 2, :], in1=rS, op=ALU.mult, inc="V")
                        p.op(V, "tensor_tensor", out=hx1[:, ci, 0:NT], in0=xnF[:, ci, 0:NT],
                             in1=wsc[:, 2, :], op=ALU.add, inc="V")
                    slot_free[par] = p.val("V")
                piece_done(nk)
                piece_done(nv)
                if is_main:
                    piece_done(nr)
            v_scan = p.val("V")
            v_x1 = p.val("V")
            t_kvr = p.val("T")

            if tile == MAIN0 - 1:
                p.op(V, "tensor_copy", out=bc_xn[:, :].unsqueeze(-1), in_=xnF[:, :, NT - 1:NT], inc="V")
                p.op(V, "tensor_copy", out=bc_xxx[:, :].unsqueeze(-1), in_=xxb[:, :, NT - 1:NT], inc="V")

            if not is_main:
                prev_xnF_free = p.val("V")
                prev_hx1_free = p.val("V")
                prev_tail_v = p.val("V")
                xxb_free_t = p.val("T")
                continue

            if tile == MAIN0:
                if bank_free["PSC"] is not None and bank_free["PSC"][0] == "S":
                    p.raw_wait(T, "S", bank_free["PSC"][1])
                for oh in range(2):
                    nrb, rb_slot, rb_w = load_piece("wr", wr_s[:, oh * PSZ:(oh + 1) * PSZ])
                    for oj4 in range(4):
                        oj = oh * 4 + oj4
                        for kk in range(NC):
                            p.op(T, "matmul", wait=[rb_w] if (oj4 == 0 and kk == 0) else None,
                                 out=PSC[:, oj:oj + 1],
                                 lhsT=pc[rb_slot][:, kk, oj4 * 128:(oj4 + 1) * 128],
                                 rhs=bc_xxx[:, kk:kk + 1], start=(kk == 0), stop=(kk == NC - 1),
                                 inc="T")
                    piece_done(nrb)
                t_rb = p.val("T")
                ecv = lambda i: ec[:, i, 0:NC]
                p.op(S, "activation", wait=[("T", t_rb)], out=ecv(0), in_=PSC[:, 0:NC],
                     func=ACT.Sigmoid, inc="S")
                s_rb = p.val("S")
                bank_free["PSC"] = ("S", s_rb)
                p.op(V, "tensor_tensor", wait=[("S", s_rb)], out=ecv(1), in0=bc_ek[:, :],
                     in1=eu[:, :], op=ALU.mult, inc="V")
                p.op(V, "tensor_tensor", out=ecv(2), in0=bc_B[:, :], in1=ecv(1), op=ALU.add, inc="V")
                p.op(V, "reciprocal", out=ecv(2), in_=ecv(2), inc="V")
                p.op(V, "tensor_tensor", out=ecv(3), in0=bc_B[:, :], in1=bc_aa[:, :], op=ALU.mult, inc="V")
                p.op(V, "tensor_tensor", out=ecv(4), in0=ecv(1), in1=bc_v[:, :], op=ALU.mult, inc="V")
                p.op(V, "tensor_tensor", out=ecv(3), in0=ecv(3), in1=ecv(4), op=ALU.add, inc="V")
                p.op(V, "tensor_tensor", out=ecv(3), in0=ecv(3), in1=ecv(2), op=ALU.mult, inc="V")
                p.op(V, "tensor_tensor", out=ecv(3), in0=ecv(3), in1=ecv(0), op=ALU.mult, inc="V")
                p.op(V, "tensor_tensor", out=ecv(3), in0=ecv(3), in1=bc_xn[:, :], op=ALU.add, inc="V")
                v_x1b = p.val("V")
                p.op(S, "activation", wait=[("V", v_x1b)], out=ecv(4), in_=ecv(3),
                     func=ACT.Square, inc="S")
                s_sq = p.val("S")
                p.raw_wait(T, "S", s_sq)
                p.op(T, "matmul", out=PSC[0:1, 16:24], lhsT=ones_sq[:, 0:1],
                     rhs=ecv(3), start=True, stop=True, inc="T")
                p.op(T, "matmul", out=PSC[0:1, 24:32], lhsT=ones_sq[:, 0:1],
                     rhs=ecv(4), start=True, stop=True, inc="T")
                t_st = p.val("T")
                p.op(V, "tensor_reduce", wait=[("T", t_st)], out=st1[0:1, 0:1],
                     in_=PSC[0:1, 16:24], axis=mybir.AxisListType.X, op=ALU.add, inc="V")
                p.op(V, "tensor_reduce", out=st1[0:1, 1:2], in_=PSC[0:1, 24:32],
                     axis=mybir.AxisListType.X, op=ALU.add, inc="V")
                p.op(V, "tensor_scalar", out=st1[0:1, 0:2], in0=st1[0:1, 0:2],
                     scalar1=1.0 / C, scalar2=None, op0=ALU.mult, inc="V")
                p.op(V, "tensor_tensor", out=st2[0:1, 0:1], in0=st1[0:1, 0:1],
                     in1=st1[0:1, 0:1], op=ALU.mult, inc="V")
                p.op(V, "tensor_tensor", out=st2[0:1, 0:1], in0=st1[0:1, 1:2],
                     in1=st2[0:1, 0:1], op=ALU.subtract, inc="V")
                v_var = p.val("V")
                p.op(S, "activation", wait=[("V", v_var)], out=st2[0:1, 1:2],
                     in_=st2[0:1, 0:1], func=ACT.Sqrt, bias=epsb[0:1, 0:1], inc="S")
                s_std = p.val("S")
                p.op(V, "reciprocal", wait=[("S", s_std)], out=st1[0:1, 1:2],
                     in_=st2[0:1, 1:2], inc="V")
                v_mu = p.val("V")
                p.raw_wait(T, "V", v_mu)
                p.op(T, "matmul", out=PSC[:, 32:34], lhsT=ones_sq[0:1, :],
                     rhs=st1[0:1, 0:2], start=True, stop=True, inc="T")
                t_bc = p.val("T")
                p.op(V, "tensor_scalar", wait=[("T", t_bc)], out=ecv(3), in0=ecv(3),
                     scalar1=PSC[:, 32:33], scalar2=None, op0=ALU.subtract, inc="V")
                p.op(V, "tensor_scalar", out=ecv(3), in0=ecv(3),
                     scalar1=PSC[:, 33:34], scalar2=None, op0=ALU.mult, inc="V")
                p.op(V, "tensor_tensor", out=ecv(3), in0=ecv(3), in1=cvec[:, 3, :], op=ALU.mult, inc="V")
                p.op(V, "tensor_tensor", out=h2car[:, :], in0=ecv(3), in1=cvec[:, 4, :], op=ALU.add, inc="V")
                bank_free["PSC"] = ("V", p.val("V"))

            v_quant = v_xq
            for grp in range(4):
                for sub in range(2):
                    n_, slot, wsp = load_piece(
                        "w1", w1_s[:, (grp * 2 + sub) * PSZ:(grp * 2 + sub + 1) * PSZ])
                    for oj4 in range(4):
                        oj8 = sub * 4 + oj4
                        bank = PSD if oj8 % 2 == 0 else PSE
                        bname = "PSD" if oj8 % 2 == 0 else "PSE"
                        w = ([wsp] + ([("V", v_xq)] if grp == 0 and sub == 0 else [])
                             if oj4 == 0 else []) + bf_wait(bname)
                        for kk in range(NC):
                            p.op(T, "matmul", wait=w if kk == 0 else None,
                                 out=bank[:, :], lhsT=pc[slot][:, kk, oj4 * 128:(oj4 + 1) * 128],
                                 rhs=xqb[:, kk, :], start=(kk == 0), stop=(kk == NC - 1), inc="T")
                        t_mm = p.val("T")
                        oj = grp * NC + oj8
                        p.op(S, "activation", wait=[("T", t_mm), ("V", v_quant)],
                             out=P2[:, oj8, :], in_=bank[:, :], func=ACT.Identity,
                             bias=b1q[:, oj:oj + 1], inc="S")
                        bank_free[bname] = ("S", p.val("S"))
                    piece_done(n_)
                s_grp = p.val("S")
                v_quant = quant_y1(P2f, hqkc[:, grp * NC:(grp + 1) * NC, :],
                                   P1f, gF[:, :, :], ("S", s_grp))
            t_w1done = p.val("T")
            p.op("sync", "dma_start", wait=[("T", t_w1done)], inc="LM", incby=16,
                 out=xqb[:, :, :], in_=mask_in[:, tile - MAIN0, :, :])
            lm_now = p.val("LM")
            for hcg in range(4):
                for oh in range(2):
                    n_, slot, wsp = load_piece(
                        "w2", w2_s[:, (hcg * 2 + oh) * PSZ:(hcg * 2 + oh + 1) * PSZ])
                    for oj4 in range(4):
                        oj = oh * 4 + oj4
                        w = None
                        if hcg == 0 and oh == 0 and oj4 == 0:
                            w = [wsp, ("V", v_quant), ("S", p.val("S"))] + bf_wait(*BNAMES)
                        elif oj4 == 0:
                            w = [wsp]
                        for hc8 in range(NC):
                            p.op(T, "matmul", wait=w if hc8 == 0 else None,
                                 out=BANKS8[oj][:, :],
                                 lhsT=pc[slot][:, hc8, oj4 * 128:(oj4 + 1) * 128],
                                 rhs=hqkc[:, hcg * NC + hc8, :], start=(hcg == 0 and hc8 == 0),
                                 stop=(hcg == 3 and hc8 == NC - 1), inc="T")
                    piece_done(n_)
            t_w2 = p.val("T")
            for oj in range(NC):
                p.op(S, "activation", wait=[("T", t_w2), ("V", v_x1)] if oj == 0 else None,
                     out=P0[:, oj, :], in_=BANKS8[oj][:, :], func=ACT.Identity,
                     bias=b2q[:, oj:oj + 1], inc="S")
            s_y2 = p.val("S")
            for b in BNAMES:
                bank_free[b] = ("S", s_y2)
            v_rm = rm_half(P0f, P2f, P1f, ("S", s_y2))
            p.op(V, "tensor_tensor", wait=[("LM", lm_now)], out=P2f, in0=P2f,
                 in1=xqb[:, :, :], op=ALU.mult, inc="V")
            v_yd = p.val("V")
            p.op(S, "activation", wait=[("V", v_yd)],
                 out=gF[:, :, :], in_=P2f, func=ACT.Silu, inc="S")
            s_g = p.val("S")

            for ci in range(NC):
                p.op(S, "activation", wait=[("V", v_yd)] if ci == 0 else None,
                     out=P0[:, ci, :], in_=hx1[:, ci, 0:NT], func=ACT.Square, inc="S")
            s_sqs = p.val("S")
            def tree8(srcf, dst_idx, w0):
                p.op(V, "tensor_tensor", wait=w0, out=P2[:, 4, :], in0=srcf(0), in1=srcf(1), op=ALU.add, inc="V")
                p.op(V, "tensor_tensor", out=P2[:, 5, :], in0=srcf(2), in1=srcf(3), op=ALU.add, inc="V")
                p.op(V, "tensor_tensor", out=P2[:, 6, :], in0=srcf(4), in1=srcf(5), op=ALU.add, inc="V")
                p.op(V, "tensor_tensor", out=P2[:, 7, :], in0=srcf(6), in1=srcf(7), op=ALU.add, inc="V")
                p.op(V, "tensor_tensor", out=P2[:, 4, :], in0=P2[:, 4, :], in1=P2[:, 5, :], op=ALU.add, inc="V")
                p.op(V, "tensor_tensor", out=P2[:, 6, :], in0=P2[:, 6, :], in1=P2[:, 7, :], op=ALU.add, inc="V")
                p.op(V, "tensor_tensor", out=P2[:, dst_idx, :], in0=P2[:, 4, :], in1=P2[:, 6, :], op=ALU.add, inc="V")
                return p.val("V")
            tree8(lambda ci: hx1[:, ci, 0:NT], 0, [("S", s_g), ("S", s_sqs)])
            v_t1 = tree8(lambda ci: P0[:, ci, :], 1, None)
            p.raw_wait(T, "V", v_t1)
            p.op(T, "matmul", out=PST[0:1, :], lhsT=ones_sq[:, 0:1], rhs=P2[:, 0, :],
                 start=True, stop=True, inc="T")
            p.op(T, "matmul", out=PSX[0:1, :], lhsT=ones_sq[:, 0:1], rhs=P2[:, 1, :],
                 start=True, stop=True, inc="T")
            t_stats = p.val("T")
            p.op(V, "tensor_scalar", wait=[("T", t_stats)], out=st1[0:1, :], in0=PST[0:1, :],
                 scalar1=1.0 / C, scalar2=None, op0=ALU.mult, inc="V")
            p.op(V, "tensor_scalar", out=st2[0:1, :], in0=PSX[0:1, :],
                 scalar1=1.0 / C, scalar2=None, op0=ALU.mult, inc="V")
            p.op(V, "tensor_tensor", out=st3[0:1, :], in0=st1[0:1, :], in1=st1[0:1, :], op=ALU.mult, inc="V")
            p.op(V, "tensor_tensor", out=st2[0:1, :], in0=st2[0:1, :], in1=st3[0:1, :], op=ALU.subtract, inc="V")
            v_var2 = p.val("V")
            p.op(S, "activation", wait=[("V", v_var2)], out=st2[0:1, :], in_=st2[0:1, :],
                 func=ACT.Sqrt, bias=epsb[0:1, 0:1], inc="S")
            s_std2 = p.val("S")
            p.op(V, "reciprocal", wait=[("S", s_std2)], out=st3[0:1, :], in_=st2[0:1, :], inc="V")
            v_rstd2 = p.val("V")
            p.raw_wait(T, "V", v_rstd2)
            p.op(T, "matmul", out=PST[:, :], lhsT=ones_sq[0:1, :], rhs=st1[0:1, :],
                 start=True, stop=True, inc="T")
            p.op(T, "matmul", out=PSX[:, :], lhsT=ones_sq[0:1, :], rhs=st3[0:1, :],
                 start=True, stop=True, inc="T")
            t_bc2 = p.val("T")
            p.op(V, "tensor_copy", wait=[("T", t_bc2)],
                 out=xnF[:, :, 0:1], in_=h2car[:, :].unsqueeze(-1), inc="V")
            for ci in range(NC):
                p.op(V, "tensor_tensor", out=P2[:, ci, :], in0=hx1[:, ci, 0:NT],
                     in1=PST[:, :], op=ALU.subtract, inc="V")
                p.op(V, "tensor_tensor", out=P2[:, ci, :], in0=P2[:, ci, :],
                     in1=PSX[:, :], op=ALU.mult, inc="V")
            v_n2 = p.val("V")
            for ci in range(NC):
                p.op(S, "activation", wait=[("V", v_n2)] if ci == 0 else None,
                     out=xnF[:, ci, 1:NT + 1], in_=P2[:, ci, :], func=ACT.Identity,
                     scale=cvec[:, 3, ci:ci + 1], bias=cvec[:, 4, ci:ci + 1], inc="S")
            s_h2 = p.val("S")
            bank_free["PST"] = ("V", v_n2)
            bank_free["PSX"] = ("V", v_n2)
            p.op(V, "tensor_copy", wait=[("S", s_h2)], out=h2car[:, :].unsqueeze(-1),
                 in_=xnF[:, :, NT:NT + 1], inc="V")
            for ci in range(NC):
                p.op(V, "tensor_scalar", wait=[("T", t_w1done)] if ci == 0 else None,
                     out=P2[:, ci, :], in0=xnF[:, ci, 1:NT + 1],
                     scalar1=omts_ap(ci), scalar2=None, op0=ALU.mult, inc="V")
                p.op(V, "scalar_tensor_tensor", out=xxb[:, ci, :], in0=xnF[:, ci, 0:NT],
                     scalar=ts_ap(ci), in1=P2[:, ci, :], op0=ALU.mult, op1=ALU.add, inc="V")
            v_xxx2 = p.val("V")
            prev_xnF_free = v_xxx2

            v_kcevac = 0
            kc_v_free = {0: 0, 1: 0}
            for pi in range(8):
                n_, slot, wsp = load_piece("wck", wck_s[:, pi * PSZ:(pi + 1) * PSZ])
                for oj4 in range(4):
                    oj = pi * 4 + oj4
                    bank = PSD if oj % 2 == 0 else PSE
                    bname = "PSD" if oj % 2 == 0 else "PSE"
                    w = ([wsp] + ([("V", v_xxx2)] if pi == 0 else [])
                         if oj4 == 0 else []) + bf_wait(bname)
                    for kk in range(NC):
                        p.op(T, "matmul", wait=w if kk == 0 else None,
                             out=bank[:, :], lhsT=pc[slot][:, kk, oj4 * 128:(oj4 + 1) * 128],
                             rhs=xxb[:, kk, :], start=(kk == 0), stop=(kk == NC - 1), inc="T")
                    t_mm = p.val("T")
                    sp = oj % 2
                    p.op(S, "activation", wait=[("T", t_mm), ("V", kc_v_free[sp])],
                         out=wsc[:, sp, :], in_=bank[:, :], func=ACT.Relu, inc="S")
                    s_r = p.val("S")
                    bank_free[bname] = ("S", s_r)
                    p.op(V, "tensor_tensor", wait=[("S", s_r)], out=hqkc[:, oj, :],
                         in0=wsc[:, sp, :], in1=wsc[:, sp, :], op=ALU.mult, inc="V")
                    kc_v_free[sp] = p.val("V")
                    v_kcevac = p.val("V")
                piece_done(n_)
            s_rc = 0
            for oh in range(2):
                n_, slot, wsp = load_piece("wcr", wcr_s[:, oh * PSZ:(oh + 1) * PSZ])
                for oj4 in range(4):
                    oj = oh * 4 + oj4
                    w = ([wsp] if oj4 == 0 else []) + bf_wait("PSC")
                    for kk in range(NC):
                        p.op(T, "matmul", wait=w if kk == 0 else None,
                             out=PSC[:, :], lhsT=pc[slot][:, kk, oj4 * 128:(oj4 + 1) * 128],
                             rhs=xxb[:, kk, :], start=(kk == 0), stop=(kk == NC - 1), inc="T")
                    t_mm = p.val("T")
                    p.op(S, "activation", wait=[("T", t_mm), ("V", v_rm if oj == 0 else 0)],
                         out=P1[:, oj, :], in_=PSC[:, :], func=ACT.Sigmoid, inc="S")
                    s_rc = p.val("S")
                    bank_free["PSC"] = ("S", s_rc)
                piece_done(n_)
            for hcg in range(4):
                for oh in range(2):
                    n_, slot, wsp = load_piece(
                        "wcv", wcv_s[:, (hcg * 2 + oh) * PSZ:(hcg * 2 + oh + 1) * PSZ])
                    for oj4 in range(4):
                        oj = oh * 4 + oj4
                        w = None
                        if hcg == 0 and oh == 0 and oj4 == 0:
                            w = [wsp, ("V", v_kcevac), ("S", p.val("S"))] + bf_wait(*BNAMES)
                        elif oj4 == 0:
                            w = [wsp]
                        for hc8 in range(NC):
                            p.op(T, "matmul", wait=w if hc8 == 0 else None,
                                 out=BANKS8[oj][:, :],
                                 lhsT=pc[slot][:, hc8, oj4 * 128:(oj4 + 1) * 128],
                                 rhs=hqkc[:, hcg * NC + hc8, :], start=(hcg == 0 and hc8 == 0),
                                 stop=(hcg == 3 and hc8 == NC - 1), inc="T")
                    piece_done(n_)
            t_vv = p.val("T")
            for oj in range(NC):
                p.op(V, "tensor_tensor", wait=[("T", t_vv), ("S", s_g)] if oj == 0 else None,
                     out=P0[:, oj, :], in0=P1[:, oj, :], in1=BANKS8[oj][:, :], op=ALU.mult, inc="V")
                p.op(V, "tensor_tensor", out=hx1[:, oj, 0:NT], in0=hx1[:, oj, 0:NT],
                     in1=P0[:, oj, :], op=ALU.add, inc="V")
                p.op(V, "tensor_tensor", out=P2[:, oj, :], in0=hx1[:, oj, 0:NT],
                     in1=gF[:, oj, :], op=ALU.mult, inc="V")
            v_sg = p.val("V")
            prev_hx1_free = v_sg
            prev_g_free = v_sg
            for b in BNAMES:
                bank_free[b] = ("V", v_sg)

            if debug_taps and tile == MAIN0:
                p.op("gpsimd", "dma_start", wait=[("V", v_sg)], inc="GO", incby=16,
                     out=taps["tp_x1"], in_=hx1[:, :, 0:NT])
                p.op("gpsimd", "dma_start", inc="GO", incby=16, out=taps["tp_g"], in_=gF[:, :, :])
                p.op("gpsimd", "dma_start", inc="GO", incby=16, out=taps["tp_h2"],
                     in_=xnF[:, :, 1:NT + 1])
                p.op("gpsimd", "dma_start", inc="GO", incby=16, out=taps["tp_s"],
                     in_=hx1[:, :, 0:NT])
                p.op("gpsimd", "dma_start", inc="GO", incby=16, out=taps["tp_hq"],
                     in_=hqkc[:, :, :])
                p.raw_wait(V, "GO", p.val("GO"))

            for s in range(NSUB):
                par = s % 2
                rsem = "LR0" if par == 0 else "LR1"
                row0 = tile * NT + s * 128
                p.op("sync", "dma_start", wait=[("V", xr_last[par]), ("GO", osub_go[par])],
                     inc=rsem, incby=16, out=xr[par][:, :], in_=x_in[row0:row0 + 128, :])
                lrw = (rsem, p.val(rsem))
                for cj in range(NC):
                    bank = PST if cj < 4 else PSX
                    col = (cj % 4) * 128
                    w = None
                    if cj == 0:
                        w = [("V", v_sg if s == 0 else xout_done.get((tile, s - 1), 0))]
                        w += bf_wait("PST", "PSX")
                    p.op(T, "transpose", wait=w,
                         out=bank[:, col:col + 128], in_=P2[:, cj, s * 128:(s + 1) * 128],
                         identity=ident[:, :], inc="T")
                t_tr = p.val("T")
                p.op(V, "tensor_tensor", wait=[("T", t_tr), lrw, ("GO", osub_go[par])],
                     out=xr[par][:, 0:512],
                     in0=xr[par][:, 0:512], in1=PST[:, :], op=ALU.add, inc="V")
                p.op(V, "tensor_tensor", out=xr[par][:, 512:1024],
                     in0=xr[par][:, 512:1024], in1=PSX[:, :], op=ALU.add, inc="V")
                xout_done[(tile, s)] = p.val("V")
                xr_last[par] = p.val("V")
                bank_free["PST"] = ("V", p.val("V"))
                bank_free["PSX"] = ("V", p.val("V"))
                orow = (tile - MAIN0) * NT + s * 128
                p.op("gpsimd", "dma_start", wait=[("V", p.val("V"))], inc="GO", incby=16,
                     out=out_main[orow:orow + 128, :], in_=xr[par][:, :])
                osub_go[par] = p.val("GO")
            prev_tail_v = p.val("V")
            xxb_free_t = p.val("T")

        p.op("sync", "dma_start", inc="LC", incby=16, out=ec[:, 9, 0:NC], in_=xe_in)
        p.op("sync", "dma_start", inc="LC", incby=16, out=ecb[:, 1, 0:NC], in_=maske_in)
        lc_e = p.val("LC")
        ecv = lambda i: ec[:, i, 0:NC]

        def ln1col(src_ap, dst_ap, wchain, affine=None):
            p.op(S, "activation", wait=wchain, out=ecv(8), in_=src_ap, func=ACT.Square, inc="S")
            s_q = p.val("S")
            p.raw_wait(T, "S", s_q)
            p.op(T, "matmul", out=PSC[0:1, 40:48], lhsT=ones_sq[:, 0:1], rhs=src_ap,
                 start=True, stop=True, inc="T")
            p.op(T, "matmul", out=PSC[0:1, 48:56], lhsT=ones_sq[:, 0:1], rhs=ecv(8),
                 start=True, stop=True, inc="T")
            t_q = p.val("T")
            p.op(V, "tensor_reduce", wait=[("T", t_q)], out=st1[0:1, 2:3], in_=PSC[0:1, 40:48],
                 axis=mybir.AxisListType.X, op=ALU.add, inc="V")
            p.op(V, "tensor_reduce", out=st1[0:1, 3:4], in_=PSC[0:1, 48:56],
                 axis=mybir.AxisListType.X, op=ALU.add, inc="V")
            p.op(V, "tensor_scalar", out=st1[0:1, 2:4], in0=st1[0:1, 2:4],
                 scalar1=1.0 / C, scalar2=None, op0=ALU.mult, inc="V")
            p.op(V, "tensor_tensor", out=st2[0:1, 2:3], in0=st1[0:1, 2:3], in1=st1[0:1, 2:3],
                 op=ALU.mult, inc="V")
            p.op(V, "tensor_tensor", out=st2[0:1, 2:3], in0=st1[0:1, 3:4], in1=st2[0:1, 2:3],
                 op=ALU.subtract, inc="V")
            v_q = p.val("V")
            p.op(S, "activation", wait=[("V", v_q)], out=st2[0:1, 3:4], in_=st2[0:1, 2:3],
                 func=ACT.Sqrt, bias=epsb[0:1, 0:1], inc="S")
            s_q2 = p.val("S")
            p.op(V, "reciprocal", wait=[("S", s_q2)], out=st1[0:1, 3:4], in_=st2[0:1, 3:4], inc="V")
            v_q2 = p.val("V")
            p.raw_wait(T, "V", v_q2)
            p.op(T, "matmul", out=PSC[:, 56:58], lhsT=ones_sq[0:1, :], rhs=st1[0:1, 2:4],
                 start=True, stop=True, inc="T")
            t_q2 = p.val("T")
            p.op(V, "tensor_scalar", wait=[("T", t_q2)], out=dst_ap, in0=src_ap,
                 scalar1=PSC[:, 56:57], scalar2=None, op0=ALU.subtract, inc="V")
            p.op(V, "tensor_scalar", out=dst_ap, in0=dst_ap,
                 scalar1=PSC[:, 57:58], scalar2=None, op0=ALU.mult, inc="V")
            if affine is not None:
                wrow, brow = affine
                p.op(V, "tensor_tensor", out=dst_ap, in0=dst_ap, in1=cvec[:, wrow, :], op=ALU.mult, inc="V")
                p.op(V, "tensor_tensor", out=dst_ap, in0=dst_ap, in1=cvec[:, brow, :], op=ALU.add, inc="V")
            return p.val("V")

        v_e0 = ln1col(ecv(9), ecv(0), [("LC", lc_e), ("V", prev_tail_v)])
        v_e1 = ln1col(ecv(0), ecv(1), [("V", v_e0)], affine=(1, 2))
        p.op(V, "tensor_scalar", out=ecv(7), in0=hcar[:, :], scalar1=sig[:, 0:1],
             scalar2=None, op0=ALU.mult, inc="V")
        p.op(V, "scalar_tensor_tensor", out=ecv(7), in0=cvec[:, 0, :], scalar=rho[:, 0:1],
             in1=ecv(7), op0=ALU.mult, op1=ALU.add, inc="V")
        p.op(V, "tensor_tensor", out=ecv(8), in0=ecv(1), in1=ecv(7), op=ALU.subtract, inc="V")
        p.op(V, "tensor_tensor", out=ecv(8), in0=ecv(8), in1=cvec[:, 0, :], op=ALU.mult, inc="V")
        p.op(V, "tensor_tensor", out=ecv(8), in0=ecv(8), in1=ecv(1), op=ALU.add, inc="V")
        p.op(V, "tensor_copy", out=bc_xxx[:, :], in_=ecv(8), inc="V")
        v_xxe = p.val("V")

        def col_mm8(skey, scratch, rhs_bf, psc_off, wait_extra=None):
            for oh in range(2):
                n_, slot, wsp = load_piece(skey, scratch[:, oh * PSZ:(oh + 1) * PSZ])
                for oj4 in range(4):
                    oj = oh * 4 + oj4
                    w = ([wsp] + (wait_extra or [] if oh == 0 else [])) if oj4 == 0 else None
                    for kk in range(NC):
                        p.op(T, "matmul", wait=w if kk == 0 else None,
                             out=PSC[:, psc_off + oj:psc_off + oj + 1],
                             lhsT=pc[slot][:, kk, oj4 * 128:(oj4 + 1) * 128],
                             rhs=rhs_bf[:, kk:kk + 1], start=(kk == 0), stop=(kk == NC - 1), inc="T")
                piece_done(n_)
            return p.val("T")

        t_ke = col_mm8("wk", wk_s, bc_xxx, 0, [("V", v_xxe), ("S", p.val("S"))])
        p.op(S, "copy", wait=[("T", t_ke)], out=ecv(2), in_=PSC[:, 0:NC], inc="S")
        t_ve = col_mm8("wv", wv_s, bc_xxx, 8)
        p.op(S, "copy", wait=[("T", t_ve)], out=ecv(3), in_=PSC[:, 8:16], inc="S")
        t_re = col_mm8("wr", wr_s, bc_xxx, 16)
        p.op(S, "activation", wait=[("T", t_re)], out=ecv(4), in_=PSC[:, 16:24],
             func=ACT.Sigmoid, inc="S")
        s_kvr = p.val("S")
        p.op(V, "tensor_scalar", wait=[("S", s_kvr)], out=ecv(7), in0=carB[:, :],
             scalar1=sig[:, 0:1], scalar2=None, op0=ALU.mult, inc="V")
        p.op(V, "tensor_scalar", out=ecv(7), in0=ecv(7), scalar1=rho[:, 0:1],
             scalar2=None, op0=ALU.add, inc="V")
        p.op(V, "tensor_scalar", out=ecv(8), in0=carA[:, :], scalar1=sig[:, 0:1],
             scalar2=None, op0=ALU.mult, inc="V")
        v_pre = p.val("V")
        p.op(S, "activation", wait=[("V", v_pre)], out=ecv(2), in_=ecv(2), func=ACT.Exp, inc="S")
        s_eke = p.val("S")
        p.op(V, "tensor_tensor", wait=[("S", s_eke)], out=ecv(2), in0=ecv(2), in1=eu[:, :],
             op=ALU.mult, inc="V")
        p.op(V, "tensor_tensor", out=ecv(5), in0=ecv(7), in1=ecv(2), op=ALU.add, inc="V")
        p.op(V, "reciprocal", out=ecv(5), in_=ecv(5), inc="V")
        p.op(V, "tensor_tensor", out=ecv(7), in0=ecv(7), in1=ecv(8), op=ALU.mult, inc="V")
        p.op(V, "tensor_tensor", out=ecv(8), in0=ecv(2), in1=ecv(3), op=ALU.mult, inc="V")
        p.op(V, "tensor_tensor", out=ecv(7), in0=ecv(7), in1=ecv(8), op=ALU.add, inc="V")
        p.op(V, "tensor_tensor", out=ecv(7), in0=ecv(7), in1=ecv(5), op=ALU.mult, inc="V")
        p.op(V, "tensor_tensor", out=ecv(7), in0=ecv(7), in1=ecv(4), op=ALU.mult, inc="V")
        p.op(V, "tensor_tensor", out=ecv(5), in0=ecv(7), in1=ecv(0), op=ALU.add, inc="V")
        v_h2e = ln1col(ecv(5), ecv(6), [("V", p.val("V"))], affine=(3, 4))
        p.op(V, "tensor_tensor", out=ecv(7), in0=ecv(6), in1=omts[:, :], op=ALU.mult, inc="V")
        p.op(V, "tensor_tensor", out=ecv(8), in0=h2car[:, :], in1=cvec[:, 0, :], op=ALU.mult, inc="V")
        p.op(V, "tensor_tensor", out=ecv(7), in0=ecv(7), in1=ecv(8), op=ALU.add, inc="V")
        p.op(V, "tensor_copy", out=bc_xxx[:, :], in_=ecv(7), inc="V")
        v_xx2e = p.val("V")
        for oh in range(2):
            n_, slot, wsp = load_piece("wcr", wcr_s[:, oh * PSZ:(oh + 1) * PSZ])
            for oj4 in range(4):
                oj = oh * 4 + oj4
                w = ([wsp] + ([("V", v_xx2e)] if oh == 0 else [])) if oj4 == 0 else None
                for kk in range(NC):
                    p.op(T, "matmul", wait=w if kk == 0 else None,
                         out=PSC[:, 24 + oj:25 + oj],
                         lhsT=pc[slot][:, kk, oj4 * 128:(oj4 + 1) * 128],
                         rhs=bc_xxx[:, kk:kk + 1], start=(kk == 0), stop=(kk == NC - 1), inc="T")
            piece_done(n_)
        t_rce = p.val("T")
        p.op(S, "activation", wait=[("T", t_rce)], out=ecv(4), in_=PSC[:, 24:32],
             func=ACT.Sigmoid, inc="S")
        s_rce = p.val("S")
        for pi in range(8):
            n_, slot, wsp = load_piece("wck", wck_s[:, pi * PSZ:(pi + 1) * PSZ])
            for oj4 in range(4):
                oj = pi * 4 + oj4
                w = ([wsp] + ([("V", v_xx2e), ("S", s_rce)] if pi == 0 else [])) if oj4 == 0 else None
                for kk in range(NC):
                    p.op(T, "matmul", wait=w if kk == 0 else None,
                         out=PSD[:, oj:oj + 1],
                         lhsT=pc[slot][:, kk, oj4 * 128:(oj4 + 1) * 128],
                         rhs=bc_xxx[:, kk:kk + 1], start=(kk == 0), stop=(kk == NC - 1), inc="T")
            piece_done(n_)
        t_kce = p.val("T")
        p.op(V, "tensor_scalar", wait=[("T", t_kce)], out=ec[:, 8, :],
             in0=PSD[:, 0:NH], scalar1=0.0, scalar2=None, op0=ALU.max, inc="V")
        p.op(V, "tensor_tensor", out=ecb[:, 0, :], in0=ec[:, 8, :], in1=ec[:, 8, :],
             op=ALU.mult, inc="V")
        v_kce = p.val("V")
        for hcg in range(4):
            for oh in range(2):
                n_, slot, wsp = load_piece(
                    "wcv", wcv_s[:, (hcg * 2 + oh) * PSZ:(hcg * 2 + oh + 1) * PSZ])
                for oj4 in range(4):
                    oj = oh * 4 + oj4
                    col = hcg * 8 + oj
                    w = ([wsp] + ([("V", v_kce)] if hcg == 0 and oh == 0 else [])) if oj4 == 0 else None
                    for hc8 in range(NC):
                        p.op(T, "matmul", wait=w if hc8 == 0 else None,
                             out=PSE[:, col:col + 1],
                             lhsT=pc[slot][:, hc8, oj4 * 128:(oj4 + 1) * 128],
                             rhs=ecb[:, 0, hcg * NC + hc8:hcg * NC + hc8 + 1],
                             start=(hc8 == 0), stop=(hc8 == NC - 1), inc="T")
                piece_done(n_)
        t_vve = p.val("T")
        p.op(S, "copy", wait=[("T", t_vve)], out=ec[:, 8, :], in_=PSE[:, 0:NH], inc="S")
        s_vve = p.val("S")
        p.op(V, "tensor_tensor", wait=[("S", s_vve)], out=ecv(6),
             in0=ec[:, 8, 0:8], in1=ec[:, 8, 8:16], op=ALU.add, inc="V")
        p.op(V, "tensor_tensor", out=ecv(7), in0=ec[:, 8, 16:24], in1=ec[:, 8, 24:32], op=ALU.add, inc="V")
        p.op(V, "tensor_tensor", out=ecv(6), in0=ecv(6), in1=ecv(7), op=ALU.add, inc="V")
        p.op(V, "tensor_tensor", out=ecv(6), in0=ecv(4), in1=ecv(6), op=ALU.mult, inc="V")
        p.op(V, "tensor_tensor", out=ecv(5), in0=ecv(5), in1=ecv(6), op=ALU.add, inc="V")
        v_se = p.val("V")
        p.op(V, "tensor_scalar", out=ec[:, 6, 0:NC].bitcast(I32), in0=ecv(0).bitcast(I32),
             scalar1=M_SBIN, scalar2=None, op0=ALU.bitwise_and, inc="V")
        p.op(V, "scalar_tensor_tensor", out=ecv(7), in0=ec[:, 6, 0:NC].bitcast(F32),
             scalar=0.125, in1=ecv(0), op0=ALU.mult, op1=ALU.add, inc="V")
        p.op(V, "tensor_scalar", out=ec[:, 6, 0:NC].bitcast(I32), in0=ecv(7).bitcast(I32),
             scalar1=M_E5M2, scalar2=None, op0=ALU.bitwise_and, inc="V")
        p.op(V, "tensor_copy", out=bc_xxx[:, :], in_=ec[:, 6, 0:NC].bitcast(F32), inc="V")
        v_xqe = p.val("V")
        for pi in range(8):
            n_, slot, wsp = load_piece("w1", w1_s[:, pi * PSZ:(pi + 1) * PSZ])
            for oj4 in range(4):
                oj = pi * 4 + oj4
                w = ([wsp] + ([("V", v_xqe)] if pi == 0 else [])) if oj4 == 0 else None
                for kk in range(NC):
                    p.op(T, "matmul", wait=w if kk == 0 else None,
                         out=PSF[:, oj:oj + 1],
                         lhsT=pc[slot][:, kk, oj4 * 128:(oj4 + 1) * 128],
                         rhs=bc_xxx[:, kk:kk + 1], start=(kk == 0), stop=(kk == NC - 1), inc="T")
            piece_done(n_)
        t_y1e = p.val("T")
        p.op(S, "copy", wait=[("T", t_y1e)], out=ec[:, 7, :], in_=PSF[:, 0:NH], inc="S")
        s_y1e = p.val("S")
        p.op(V, "tensor_tensor", wait=[("S", s_y1e)], out=ec[:, 7, :], in0=ec[:, 7, :],
             in1=b1q[:, :], op=ALU.add, inc="V")
        p.op(V, "tensor_scalar", out=ec[:, 8, :].bitcast(I32), in0=ec[:, 7, :].bitcast(I32),
             scalar1=M_SBIN, scalar2=None, op0=ALU.bitwise_and, inc="V")
        p.op(V, "scalar_tensor_tensor", out=ec[:, 6, :], in0=ec[:, 8, :].bitcast(F32),
             scalar=0.00390625, in1=ec[:, 7, :], op0=ALU.mult, op1=ALU.add, inc="V")
        p.op(V, "tensor_scalar", out=ec[:, 8, :].bitcast(I32), in0=ec[:, 6, :].bitcast(I32),
             scalar1=M_BF16, scalar2=None, op0=ALU.bitwise_and, inc="V")
        p.op(V, "tensor_scalar", out=ec[:, 7, :].bitcast(I32), in0=ec[:, 8, :].bitcast(I32),
             scalar1=M_SBIN, scalar2=None, op0=ALU.bitwise_and, inc="V")
        p.op(V, "tensor_scalar", out=ec[:, 6, :].bitcast(I32), in0=ec[:, 8, :].bitcast(I32),
             scalar1=M_TIE, scalar2=None, op0=ALU.bitwise_and, inc="V")
        p.op(V, "tensor_scalar", out=ec[:, 6, :], in0=ec[:, 6, :].bitcast(I32),
             scalar1=V_TIE, scalar2=None, op0=ALU.is_equal, inc="V")
        p.op(V, "tensor_scalar", out=ec[:, 6, :], in0=ec[:, 6, :], scalar1=-0.25,
             scalar2=0.125, op0=ALU.mult, op1=ALU.add, inc="V")
        p.op(V, "tensor_tensor", out=ec[:, 6, :], in0=ec[:, 6, :], in1=ec[:, 7, :].bitcast(F32),
             op=ALU.mult, inc="V")
        p.op(V, "tensor_tensor", out=ec[:, 6, :], in0=ec[:, 6, :], in1=ec[:, 8, :].bitcast(F32),
             op=ALU.add, inc="V")
        p.op(V, "tensor_scalar", out=ec[:, 8, :].bitcast(I32), in0=ec[:, 6, :].bitcast(I32),
             scalar1=M_E5M2, scalar2=None, op0=ALU.bitwise_and, inc="V")
        p.op(V, "tensor_scalar", out=ecb[:, 0, :], in0=ec[:, 8, :].bitcast(F32),
             scalar1=0.0, scalar2=None, op0=ALU.max, inc="V")
        v_hqe = p.val("V")
        for hcg in range(4):
            for oh in range(2):
                n_, slot, wsp = load_piece(
                    "w2", w2_s[:, (hcg * 2 + oh) * PSZ:(hcg * 2 + oh + 1) * PSZ])
                for oj4 in range(4):
                    oj = oh * 4 + oj4
                    col = NH + hcg * 8 + oj
                    w = ([wsp] + ([("V", v_hqe)] if hcg == 0 and oh == 0 else [])) if oj4 == 0 else None
                    for hc8 in range(NC):
                        p.op(T, "matmul", wait=w if hc8 == 0 else None,
                             out=PSF[:, col:col + 1],
                             lhsT=pc[slot][:, hc8, oj4 * 128:(oj4 + 1) * 128],
                             rhs=ecb[:, 0, hcg * NC + hc8:hcg * NC + hc8 + 1],
                             start=(hc8 == 0), stop=(hc8 == NC - 1), inc="T")
                piece_done(n_)
        t_y2e = p.val("T")
        p.op(S, "copy", wait=[("T", t_y2e)], out=ec[:, 8, :], in_=PSF[:, NH:NH + NH], inc="S")
        s_y2e = p.val("S")
        p.op(V, "tensor_tensor", wait=[("S", s_y2e)], out=ecv(7),
             in0=ec[:, 8, 0:8], in1=ec[:, 8, 8:16], op=ALU.add, inc="V")
        p.op(V, "tensor_tensor", out=ecv(8), in0=ec[:, 8, 16:24], in1=ec[:, 8, 24:32], op=ALU.add, inc="V")
        p.op(V, "tensor_tensor", out=ecv(7), in0=ecv(7), in1=ecv(8), op=ALU.add, inc="V")
        p.op(V, "tensor_tensor", out=ecv(7), in0=ecv(7), in1=b2q[:, 0:NC],
             op=ALU.add, inc="V")
        p.op(V, "tensor_scalar", out=ec[:, 8, 0:NC].bitcast(I32), in0=ecv(7).bitcast(I32),
             scalar1=M_SBIN, scalar2=None, op0=ALU.bitwise_and, inc="V")
        p.op(V, "scalar_tensor_tensor", out=ecv(6), in0=ec[:, 8, 0:NC].bitcast(F32),
             scalar=0.00390625, in1=ecv(7), op0=ALU.mult, op1=ALU.add, inc="V")
        p.op(V, "tensor_scalar", out=ecv(6).bitcast(I32), in0=ecv(6).bitcast(I32),
             scalar1=M_BF16, scalar2=None, op0=ALU.bitwise_and, inc="V")
        p.op(V, "tensor_tensor", out=ecv(6), in0=ecv(6), in1=ecb[:, 1, 0:NC], op=ALU.mult, inc="V")
        v_yde = p.val("V")
        p.op(S, "activation", wait=[("V", v_yde)], out=ecv(6), in_=ecv(6), func=ACT.Silu, inc="S")
        s_ge = p.val("S")
        p.op(V, "tensor_tensor", wait=[("S", s_ge)], out=ecv(6), in0=ecv(6), in1=ecv(5),
             op=ALU.mult, inc="V")
        p.op(V, "tensor_tensor", out=ecv(6), in0=ecv(6), in1=ecv(9), op=ALU.add, inc="V")
        v_oute = p.val("V")
        p.op("gpsimd", "dma_start", wait=[("V", v_oute)], inc="GO", incby=16,
             out=out_extra, in_=ecv(6))
        if debug_taps:
            p.op("gpsimd", "dma_start", wait=[("V", v_oute)], inc="GO", incby=16,
                 out=taps["tp_ec"], in_=ec[:, :, :])
            p.op("gpsimd", "dma_start", inc="GO", incby=16, out=taps["tp_carB"], in_=carB[:, :])
            p.op("gpsimd", "dma_start", inc="GO", incby=16, out=taps["tp_carA"], in_=carA[:, :])
            p.op("gpsimd", "dma_start", inc="GO", incby=16, out=taps["tp_hcar"], in_=hcar[:, :])
            p.op("gpsimd", "dma_start", inc="GO", incby=16, out=taps["tp_h2car"], in_=h2car[:, :])

        p.raw_wait("gpsimd", "GO", p.val("GO"))

        with nc.Block() as block:
            @block.sync
            def _(eng):
                for fn in p.lists["sync"]:
                    fn(eng)

            @block.gpsimd
            def _(eng):
                for fn in p.lists["gpsimd"]:
                    fn(eng)

            @block.vector
            def _(eng):
                for fn in p.lists["vector"]:
                    fn(eng)

            @block.scalar
            def _(eng):
                for fn in p.lists["scalar"]:
                    fn(eng)

            @block.tensor
            def _(eng):
                for fn in p.lists["tensor"]:
                    fn(eng)

    return nc


def prep_core_inputs(ins, mask_scaled, core):
    import ml_dtypes
    BF = ml_dtypes.bfloat16
    F = np.float32
    b, half = core >> 1, core & 1
    x = np.asarray(ins["L0_attn_x"][b], F)
    if half == 0:
        x_buf = np.concatenate([x[0:2048], x[0:2048]], axis=0)
        xe = x[2048]
        mk = mask_scaled[b, 0:2048]
        mke = mask_scaled[b, 2048]
    else:
        x_buf = x
        xe = x[0]
        mk = mask_scaled[b, 2048:4096]
        mke = mask_scaled[b, 0]

    def fm_vec(v):
        return np.ascontiguousarray(np.asarray(v, F).reshape(NC, 128).T)

    def fm_mat(wT, nch):
        K, O = wT.shape
        return np.ascontiguousarray(wT.reshape(nch, 128, O).transpose(1, 0, 2))

    def pm_osplit(wT, nch):
        fm = fm_mat(wT, nch)
        O = fm.shape[2]
        return np.ascontiguousarray(fm.reshape(128, nch, O // 512, 512)
                                    .transpose(0, 2, 1, 3))

    def pm_hsplit(wT):
        fm = fm_mat(wT, 32)
        return np.ascontiguousarray(fm.reshape(128, 4, 8, 2, 512)
                                    .transpose(0, 1, 3, 2, 4).reshape(128, 8, 8, 512))

    cvec = np.zeros((128, 8, NC), F)
    cvec[:, 0] = fm_vec(ins["L0_attn_time_shift"].reshape(-1))
    cvec[:, 1] = fm_vec(ins["L0_attn_ln1_w"])
    cvec[:, 2] = fm_vec(ins["L0_attn_ln1_b"])
    cvec[:, 3] = fm_vec(ins["L0_attn_ln2_w"])
    cvec[:, 4] = fm_vec(ins["L0_attn_ln2_b"])
    cvec[:, 5] = fm_vec(ins["L0_attn_time_first"])

    return {
        "x_in": np.ascontiguousarray(x_buf),
        "xe_in": fm_vec(xe),
        "mask_in": np.ascontiguousarray(
            mk.reshape(4, NT, NC, 128).transpose(3, 0, 2, 1)).astype(BF),
        "maske_in": fm_vec(mke).astype(BF),
        "ident_in": np.eye(128, dtype=F),
        "rho_in": np.full((128, 1), float(half), F),
        "sig_in": np.full((128, 1), 1.0 - float(half), F),
        "cvec_in": cvec,
        "b1_in": np.ascontiguousarray(np.asarray(ins["L0_ffn_b1"], F).reshape(NH, 128).T),
        "b2_in": fm_vec(ins["L0_ffn_b2"]),
        "wkT_in": pm_osplit(np.asarray(ins["L0_attn_Wk"], F).T.copy(), NC),
        "wvT_in": pm_osplit(np.asarray(ins["L0_attn_Wv"], F).T.copy(), NC),
        "wrT_in": pm_osplit(np.asarray(ins["L0_attn_Wr"], F).T.copy(), NC),
        "wcrT_in": pm_osplit(np.asarray(ins["L0_attn_Wcr"], F).T.copy(), NC),
        "wckT_in": pm_osplit(np.asarray(ins["L0_attn_Wck"], F).T.copy(), NC),
        "wcvT_in": pm_hsplit(np.asarray(ins["L0_attn_Wcv"], F).T.copy()),
        "w1T_in": pm_osplit(np.asarray(ins["L0_ffn_W1"], F).T.copy(), NC),
        "w2T_in": pm_hsplit(np.asarray(ins["L0_ffn_W2"], F).T.copy()),
    }


def make_mask():
    import jax
    with jax.default_device(jax.devices("cpu")[0]):
        keep = np.asarray(jax.random.bernoulli(
            jax.random.key(42, impl="threefry2x32"), 0.8, (4, 4096, C)))
    return keep.astype(np.float32) * np.float32(1.25)


def assemble(results):
    out = np.empty((4, 4096, C), np.float32)
    for b in range(4):
        even = results[2 * b]
        odd = results[2 * b + 1]
        out[b, 0] = np.ascontiguousarray(odd["out_extra"].T).reshape(-1)
        out[b, 1:2048] = even["out_main"][1:]
        out[b, 2048] = np.ascontiguousarray(even["out_extra"].T).reshape(-1)
        out[b, 2049:4096] = odd["out_main"][1:]
    return out


_NC_CACHE = {}


def kernel(**inputs):
    mask_scaled = make_mask()
    if "nc" not in _NC_CACHE:
        _NC_CACHE["nc"] = _build(debug_taps=False)
    nc = _NC_CACHE["nc"]
    in_maps = [prep_core_inputs(inputs, mask_scaled, core) for core in range(8)]
    res = run_bass_kernel_spmd(nc, in_maps, core_ids=list(range(8)))
    return assemble(res.results)
